# version 01 (final)
# baseline (speedup 1.0000x reference)
import numpy as np
import ml_dtypes
from contextlib import ExitStack

import concourse.bass as bass
import concourse.mybir as mybir
from concourse.bass_utils import run_bass_kernel_spmd

F32 = mybir.dt.float32
BF16 = mybir.dt.bfloat16
AF = mybir.ActivationFunctionType
ALU = mybir.AluOpType
AXX = mybir.AxisListType.X

B_, T_, D_, H_, DH_ = 4, 4096, 1024, 16, 64
NCORE = 8
SEG = 512
NSEG = T_ // SEG
CONTEXT = 250

bf = ml_dtypes.bfloat16


def host_consts():
    c = {}
    i32 = np.arange(32)
    freqs = 10000.0 ** (-i32 / 32.0)
    t = np.arange(T_)
    ang = t[None, :] * freqs[:, None]
    c["cos128"] = np.tile(np.cos(ang), (4, 1)).astype(np.float32)
    c["sin128"] = np.tile(np.sin(ang), (4, 1)).astype(np.float32)
    Bm = np.zeros((128, 128), np.float32)
    for blk in range(2):
        for i in range(32):
            e = blk * 64 + i
            o = blk * 64 + 32 + i
            Bm[o, e] = -1.0
            Bm[e, o] = +1.0
    c["ropeB"] = Bm.astype(bf)
    c["ident_b"] = np.eye(128, dtype=bf)
    c["ident_f"] = np.eye(128, dtype=np.float32)
    m = np.zeros((128, 7 * SEG), np.float32)
    for ai, a in enumerate(range(-3, 4)):
        p = np.arange(128)[:, None]
        j = np.arange(SEG)[None, :]
        d = j - 128 * a - p
        m[:, ai * SEG:(ai + 1) * SEG] = ((d >= 0) & (d < CONTEXT)).astype(np.float32)
    c["maskL0"] = m.astype(bf)
    return c


def l0_inproj_perm():
    idx = []
    for sec in range(2):
        for h in range(H_):
            hb = sec * 1024 + h * 64
            idx += [hb + 2 * i for i in range(32)]
            idx += [hb + 2 * i + 1 for i in range(32)]
    idx += list(range(2048, 3072))
    return np.array(idx)


ENGS = ("sync", "tensor", "vector", "scalar", "gpsimd")


class Bld:
    ENG_KIND = {"tensor": "pe", "vector": "dve", "scalar": "act", "gpsimd": "gp"}

    def __init__(self, nc):
        self.nc = nc
        self.q = {e: [] for e in ENGS}
        self.cnt = {}
        self.last_w = {}
        self.last_r = {}
        self.waited = {e: {} for e in ENGS}
        self.sems = {}
        self.nops = 0

    def op(self, eng, fn, reads=(), writes=(), dma=False):
        if dma or eng == "sync":
            kind = "dma_" + (writes[0] if writes else "misc")
            step = 16
        else:
            kind = self.ENG_KIND[eng]
            step = 1
        need = {}
        reads = list(reads)
        writes = list(writes)
        for k in list(reads):
            if k.startswith("pb"):
                reads.remove(k)
                writes.append(k)
        for k in reads:
            lw = self.last_w.get(k)
            if lw:
                need[lw[0]] = max(need.get(lw[0], 0), lw[1])
        for k in writes:
            lw = self.last_w.get(k)
            if lw:
                need[lw[0]] = max(need.get(lw[0], 0), lw[1])
            for kk, v in self.last_r.get(k, {}).items():
                need[kk] = max(need.get(kk, 0), v)
        for kk, v in sorted(need.items()):
            if self.waited[eng].get(kk, 0) >= v:
                continue
            self.q[eng].append(lambda E, kk=kk, v=v: E.wait_ge(self.sems[kk], v))
            self.waited[eng][kk] = v
        self.cnt[kind] = self.cnt.get(kind, 0) + step
        myval = self.cnt[kind]
        self.q[eng].append(lambda E, fn=fn, kind=kind, step=step:
                           fn(E).then_inc(self.sems[kind], step))
        for k in writes:
            self.last_w[k] = (kind, myval)
            self.last_r[k] = {}
        for k in reads:
            self.last_r.setdefault(k, {})[kind] = myval
        self.nops += 1
        return (kind, myval)

    def barrier_wait_all(self, eng):
        for kk, v in self.cnt.items():
            if v > self.waited[eng].get(kk, 0):
                self.q[eng].append(lambda E, kk=kk, v=v: E.wait_ge(self.sems[kk], v))
                self.waited[eng][kk] = v

    def finalize(self, stack):
        for k in self.cnt:
            self.sems[k] = stack.enter_context(self.nc.semaphore("s_" + k))
        qq = self.q
        with self.nc.Block() as block:
            @block.sync
            def _(e):
                for fn in qq["sync"]:
                    fn(e)

            @block.tensor
            def _(e):
                for fn in qq["tensor"]:
                    fn(e)

            @block.vector
            def _(e):
                for fn in qq["vector"]:
                    fn(e)

            @block.scalar
            def _(e):
                for fn in qq["scalar"]:
                    fn(e)

            @block.gpsimd
            def _(e):
                for fn in qq["gpsimd"]:
                    fn(e)


def sb(stack, nc, name, shape, dt):
    return stack.enter_context(nc.sbuf_tensor(name, shape, dt))


def ps(stack, nc, name, shape, dt=F32):
    return stack.enter_context(nc.psum_tensor(name, shape, dt))


def build_l0(nc, stack, nseg=NSEG, upto=99):
    b = Bld(nc)
    x0 = nc.dram_tensor("x0", [T_, D_], F32, kind="ExternalInput").ap()
    win = nc.dram_tensor("w_inproj", [3 * D_, D_], F32, kind="ExternalInput").ap()
    wout = nc.dram_tensor("w_outproj", [D_, D_], F32, kind="ExternalInput").ap()
    wffn = nc.dram_tensor("w_ffn", [D_, D_], F32, kind="ExternalInput").ap()
    cosd = nc.dram_tensor("cos128", [128, T_], F32, kind="ExternalInput").ap()
    sind = nc.dram_tensor("sin128", [128, T_], F32, kind="ExternalInput").ap()
    ropeBd = nc.dram_tensor("ropeB", [128, 128], BF16, kind="ExternalInput").ap()
    identbd = nc.dram_tensor("ident_b", [128, 128], BF16, kind="ExternalInput").ap()
    identfd = nc.dram_tensor("ident_f", [128, 128], F32, kind="ExternalInput").ap()
    maskd = nc.dram_tensor("maskL0", [128, 7 * SEG], BF16, kind="ExternalInput").ap()
    x2out = nc.dram_tensor("x2", [T_, D_], F32, kind="ExternalOutput").ap()

    WT = sb(stack, nc, "WT", [128, 8 * 3072], BF16)
    WoT = sb(stack, nc, "WoT", [128, 8 * 1024], BF16)
    wtsT = sb(stack, nc, "wtsT", [128, 8 * 1024], BF16)
    kring = sb(stack, nc, "kring", [128, 8 * 1024], BF16)
    vring = sb(stack, nc, "vring", [128, 8 * 1040], BF16)
    maskL0 = sb(stack, nc, "maskL0s", [128, 7 * SEG], BF16)
    cosS = sb(stack, nc, "cosS", [128, SEG], F32)
    sinS = sb(stack, nc, "sinS", [128, SEG], F32)
    ropeB = sb(stack, nc, "ropeBs", [128, 128], BF16)
    identb = sb(stack, nc, "identbs", [128, 128], BF16)
    identf = sb(stack, nc, "identfs", [128, 128], F32)
    onesf = sb(stack, nc, "onesf", [128, 128], F32)
    negC = sb(stack, nc, "negC", [128, 16], F32)
    kmaxr = sb(stack, nc, "kmaxr", [128, 16], F32)
    duo = sb(stack, nc, "duo", [128, 32], F32)
    crow = sb(stack, nc, "crow", [1, 64], F32)
    xseg = sb(stack, nc, "xseg", [128, 4 * 1024], F32)
    x1seg = sb(stack, nc, "x1seg", [128, 4 * 1024], F32)
    htok = sb(stack, nc, "htok", [128, 4 * 1024], BF16)
    hfm = sb(stack, nc, "hfm", [128, 8 * SEG], BF16)
    qfm = sb(stack, nc, "qfm", [128, 8 * SEG], BF16)
    AVN = sb(stack, nc, "AVN", [128, 8 * SEG], BF16)
    tcts = sb(stack, nc, "tcts", [128, 2 * SEG], BF16)
    Pbuf = sb(stack, nc, "Pbuf", [128, 2 * SEG], BF16)
    Pm = sb(stack, nc, "Pmbuf", [128, 2 * SEG], BF16)
    lnst = sb(stack, nc, "lnst", [128, 16], F32)
    sqb = sb(stack, nc, "sqb", [128, SEG], F32)
    rls = sb(stack, nc, "rls", [128, SEG], F32)

    pb = [ps(stack, nc, f"pb{i}", [128, 512], F32) for i in range(8)]
    pb4b = pb[4].bitcast(BF16)
    pb5b = pb[5].bitcast(BF16)

    tsl = [0]

    def dma(eng, out, in_, reads, writes):
        b.op(eng, lambda E: E.dma_start(out=out, in_=in_), reads=reads, writes=writes, dma=True)

    def transpose_b(src_ap, rkeys, dst_ap, wkeys):
        sl_ = tsl[0]
        tsl[0] ^= 1
        view = pb4b if sl_ == 0 else pb5b
        pk = "pb4" if sl_ == 0 else "pb5"
        b.op("tensor", lambda E, view=view: E.transpose(view[:, 0:128], src_ap, identb[:]),
             reads=list(rkeys) + ["identb"], writes=[pk])
        b.op("vector", lambda E, view=view: E.tensor_copy(dst_ap, view[:, 0:128]),
             reads=[pk], writes=list(wkeys))

    dma("sync", cosS[:], cosd[:, 0:SEG], [], ["cosS"])
    dma("sync", sinS[:], sind[:, 0:SEG], [], ["sinS"])
    dma("sync", ropeB[:], ropeBd, [], ["ropeB"])
    dma("sync", identb[:], identbd, [], ["identb"])
    dma("sync", identf[:], identfd, [], ["identf"])
    dma("sync", maskL0[:], maskd, [], ["maskL0"])
    b.op("vector", lambda E: E.memset(onesf[:], 1.0), writes=["onesf"])
    b.op("vector", lambda E: E.memset(kmaxr[:], 0.0), writes=["kmaxr"])
    b.op("vector", lambda E: E.memset(
        bass.AP(vring, 64, [[8 * 1040, 128], [1040, 8], [65, 16], [1, 1]]), 1.0),
        writes=["vr_ones"])

    if upto < 2:
        b.barrier_wait_all("gpsimd")
        return b
    with ExitStack() as st2:
        wstg = sb(st2, nc, "wstg", [128, 1024], BF16)
        wf32 = sb(st2, nc, "wf32", [128, 1024], F32)
        for ct in range(24):
            dma("gpsimd", wstg[:], win[128 * ct:128 * (ct + 1), :], [], ["wstg"])
            for ci in range(8):
                transpose_b(wstg[:, 128 * ci:128 * (ci + 1)], ["wstg"],
                            WT[:, 3072 * ci + 128 * ct: 3072 * ci + 128 * (ct + 1)], ["WT"])
        for ct in range(8):
            dma("gpsimd", wstg[:], wout[128 * ct:128 * (ct + 1), :], [], ["wstg"])
            for ci in range(8):
                transpose_b(wstg[:, 128 * ci:128 * (ci + 1)], ["wstg"],
                            WoT[:, 1024 * ci + 128 * ct: 1024 * ci + 128 * (ct + 1)], ["WoT"])
        if upto < 3:
            b.barrier_wait_all("gpsimd")
            return b
        for ct in range(8):
            dma("sync", wf32[:], wffn[128 * ct:128 * (ct + 1), :], [], ["wf32"])
            if ct == 0:
                b.op("vector", lambda E: E.reduce_sum(lnst[:, 10:11], wf32[:], axis=AXX,
                                                      apply_absolute_value=True),
                     reads=["wf32"], writes=["w_acc"])
            else:
                b.op("vector", lambda E: E.reduce_sum(lnst[:, 11:12], wf32[:], axis=AXX,
                                                      apply_absolute_value=True),
                     reads=["wf32"], writes=["w_tmp"])
                b.op("vector", lambda E: E.tensor_tensor(lnst[:, 10:11], lnst[:, 10:11],
                                                         lnst[:, 11:12], op=ALU.add),
                     reads=["w_acc", "w_tmp"], writes=["w_acc"])
        b.op("tensor", lambda E: E.matmul(pb[5][0:1, 0:1], lnst[:, 10:11], onesf[:, 0:1],
                                          start=True, stop=True),
             reads=["w_acc", "onesf"], writes=["pb5"])
        b.op("vector", lambda E: E.tensor_scalar(crow[0:1, 48:49], pb[5][0:1, 0:1],
                                                 1.0 / (1024.0 * 1024.0), 1e-5, ALU.mult, ALU.max),
             reads=["pb5"], writes=["crow_s"])
        b.op("tensor", lambda E: E.matmul(pb[5][:, 0:1], onesf[0:1, :], crow[0:1, 48:49],
                                          start=True, stop=True),
             reads=["crow_s", "onesf"], writes=["pb5"])
        b.op("vector", lambda E: E.tensor_copy(lnst[:, 8:9], pb[5][:, 0:1]),
             reads=["pb5"], writes=["w_s"])
        b.op("vector", lambda E: E.tensor_scalar(lnst[:, 9:10], lnst[:, 8:9], 0.5, None, ALU.mult),
             reads=["w_s"], writes=["w_h"])
        for ct in range(8):
            dma("sync", wf32[:], wffn[128 * ct:128 * (ct + 1), :], [], ["wf32"])
            b.op("scalar", lambda E: E.activation(sqb[:, 0:512], wf32[:, 0:512], AF.Sign),
                 reads=["wf32"], writes=["w_sg0"])
            b.op("scalar", lambda E: E.activation(rls[:, 0:512], wf32[:, 512:1024], AF.Sign),
                 reads=["wf32"], writes=["w_sg1"])
            b.op("scalar", lambda E: E.activation(wf32[:], wf32[:], AF.Abs),
                 reads=["wf32", "w_sg0", "w_sg1"], writes=["wf32"])
            b.op("vector", lambda E: E.tensor_scalar(wf32[:], wf32[:], lnst[:, 9:10], None, ALU.is_ge),
                 reads=["wf32", "w_h"], writes=["wf32"])
            b.op("vector", lambda E: E.scalar_tensor_tensor(wstg[:, 0:512], wf32[:, 0:512],
                                                            lnst[:, 8:9], sqb[:, 0:512],
                                                            ALU.mult, ALU.mult),
                 reads=["wf32", "w_s", "w_sg0"], writes=["wstg"])
            b.op("vector", lambda E: E.scalar_tensor_tensor(wstg[:, 512:1024], wf32[:, 512:1024],
                                                            lnst[:, 8:9], rls[:, 0:512],
                                                            ALU.mult, ALU.mult),
                 reads=["wf32", "w_s", "w_sg1"], writes=["wstg"])
            for ci in range(8):
                transpose_b(wstg[:, 128 * ci:128 * (ci + 1)], ["wstg"],
                            wtsT[:, 1024 * ci + 128 * ct: 1024 * ci + 128 * (ct + 1)], ["wtsT"])

    def layernorm(src, srckey, dst, dstkey):
        for tt in range(4):
            xa = src[:, 1024 * tt:1024 * (tt + 1)]
            b.op("vector", lambda E, xa=xa: E.reduce_sum(lnst[:, 0:1], xa, axis=AXX, negate=True),
                 reads=[srckey], writes=["ln_negsum"])
            b.op("vector", lambda E: E.tensor_scalar(lnst[:, 1:2], lnst[:, 0:1],
                                                     1.0 / 1024.0, None, ALU.mult),
                 reads=["ln_negsum"], writes=["ln_negmean"])
            b.op("scalar", lambda E, xa=xa: E.activation(pb[3][:], xa[:, 0:512], AF.Square,
                                                         accum_out=lnst[:, 2:3]),
                 reads=[srckey], writes=["pb3", "ln_sq0"])
            b.op("scalar", lambda E, xa=xa: E.activation(pb[3][:], xa[:, 512:1024], AF.Square,
                                                         accum_out=lnst[:, 3:4]),
                 reads=[srckey, "pb3"], writes=["pb3", "ln_sq1"])
            b.op("vector", lambda E: E.tensor_tensor(lnst[:, 2:3], lnst[:, 2:3],
                                                     lnst[:, 3:4], op=ALU.add),
                 reads=["ln_sq0", "ln_sq1"], writes=["ln_sq0"])
            b.op("vector", lambda E: E.tensor_scalar(lnst[:, 4:5], lnst[:, 1:2],
                                                     lnst[:, 1:2], 1e-5, ALU.mult, ALU.subtract),
                 reads=["ln_negmean"], writes=["ln_nm2"])
            b.op("vector", lambda E: E.scalar_tensor_tensor(lnst[:, 5:6], lnst[:, 2:3],
                                                            1.0 / 1024.0, lnst[:, 4:5],
                                                            ALU.mult, ALU.subtract),
                 reads=["ln_sq0", "ln_nm2"], writes=["ln_var"])
            b.op("scalar", lambda E: E.activation(lnst[:, 6:7], lnst[:, 5:6], AF.Sqrt),
                 reads=["ln_var"], writes=["ln_std"])
            b.op("vector", lambda E: E.reciprocal(lnst[:, 7:8], lnst[:, 6:7]),
                 reads=["ln_std"], writes=["ln_rstd"])
            b.op("vector", lambda E, xa=xa, tt=tt: E.tensor_scalar(
                dst[:, 1024 * tt:1024 * (tt + 1)], xa, lnst[:, 1:2], lnst[:, 7:8],
                ALU.add, ALU.mult),
                reads=[srckey, "ln_negmean", "ln_rstd"], writes=[dstkey])

    def transpose_h():
        for tt in range(4):
            for ci in range(8):
                transpose_b(htok[:, 1024 * tt + 128 * ci: 1024 * tt + 128 * (ci + 1)], ["htok"],
                            hfm[:, SEG * ci + 128 * tt: SEG * ci + 128 * (tt + 1)], ["hfm"])

    def mm_fm_tok(fmbuf, fmkey, rhsbuf, rhskey, rhs_pitch, resid_src, resid_key, out, outkey):
        for tt in range(4):
            for chf in range(2):
                pA = pb[1] if (tt * 2 + chf) % 2 == 0 else pb[7]
                pAk = "pb1" if (tt * 2 + chf) % 2 == 0 else "pb7"

                def mmo(E, pA=pA, tt=tt, chf=chf):
                    last = None
                    for ci in range(8):
                        last = E.matmul(
                            pA[:], fmbuf[:, SEG * ci + 128 * tt: SEG * ci + 128 * (tt + 1)],
                            rhsbuf[:, rhs_pitch * ci + 512 * chf: rhs_pitch * ci + 512 * (chf + 1)],
                            start=(ci == 0), stop=(ci == 7))
                    return last
                b.op("tensor", mmo, reads=[fmkey, rhskey], writes=[pAk])
                off = 1024 * tt + 512 * chf
                b.op("vector", lambda E, pA=pA, off=off: E.tensor_tensor(
                    out[:, off:off + 512], pA[:], resid_src[:, off:off + 512], op=ALU.add),
                    reads=[pAk, resid_key], writes=[outkey])

    for s in range(nseg):
        q0 = s * SEG
        sl = s % 2
        for tt in range(4):
            dma("sync", xseg[:, 1024 * tt:1024 * (tt + 1)],
                x0[q0 + 128 * tt: q0 + 128 * (tt + 1), :], [], ["xseg"])
        if s > 0:
            dma("sync", cosS[:], cosd[:, q0:q0 + SEG], [], ["cosS"])
            dma("sync", sinS[:], sind[:, q0:q0 + SEG], [], ["sinS"])

        if upto < 4:
            continue
        layernorm(xseg, "xseg", htok, "htok")
        transpose_h()

        if upto < 5:
            continue
        for ct in range(16):
            pA = pb[0] if ct % 2 == 0 else pb[6]
            pAk = "pb0" if ct % 2 == 0 else "pb6"
            pB = pb[1] if ct % 2 == 0 else pb[7]
            pBk = "pb1" if ct % 2 == 0 else "pb7"
            wcol = 128 * ct

            def mmproj(E, pA=pA, wcol=wcol):
                last = None
                for ci in range(8):
                    last = E.matmul(pA[:], WT[:, 3072 * ci + wcol: 3072 * ci + wcol + 128],
                                    hfm[:, SEG * ci: SEG * (ci + 1)],
                                    start=(ci == 0), stop=(ci == 7))
                return last
            b.op("tensor", mmproj, reads=["WT", "hfm"], writes=[pAk])
            if upto == 51:
                b.op("vector", lambda E, pA=pA, ct=ct: E.tensor_copy(
                    qfm[:, SEG * (ct % 8):SEG * (ct % 8 + 1)], pA[:]),
                    reads=[pAk], writes=["qfm"])
                continue
            b.op("vector", lambda E, pA=pA: E.tensor_tensor(tcts[:, 0:SEG], pA[:], cosS[:], op=ALU.mult),
                 reads=[pAk, "cosS"], writes=["tc"])
            b.op("vector", lambda E, pA=pA: E.tensor_tensor(tcts[:, SEG:2 * SEG], pA[:], sinS[:], op=ALU.mult),
                 reads=[pAk, "sinS"], writes=["ts"])
            if upto >= 53:
                b.op("scalar", lambda E, pA=pA: E.activation(sqb[:], pA[:], AF.Square),
                     reads=[pAk], writes=["sqb"])

            def mmnorm(E, ct=ct):
                last = None
                for hh in range(2):
                    for qs in range(4):
                        h2 = 2 * (ct % 8) + hh
                        col = (0 if ct < 8 else 64) + 4 * h2 + qs
                        last = E.matmul(pb[5][:, col:col + 1],
                                        sqb[64 * hh:64 * (hh + 1), 128 * qs:128 * (qs + 1)],
                                        onesf[64 * hh:64 * (hh + 1), 0:1],
                                        start=True, stop=True)
                return last
            if upto >= 53:
                b.op("tensor", mmnorm, reads=["sqb", "onesf"], writes=["pb5"])

            def mmrope(E, pB=pB):
                E.matmul(pB[:], identb[:], tcts[:, 0:SEG], start=True, stop=False)
                return E.matmul(pB[:], ropeB[:], tcts[:, SEG:2 * SEG], start=False, stop=True)
            b.op("tensor", mmrope, reads=["tc", "ts", "identb", "ropeB"], writes=[pBk])
            if ct < 8:
                dst, dkey = qfm[:, SEG * ct:SEG * (ct + 1)], "qfm"
            else:
                ci = ct - 8
                dst = kring[:, 1024 * ci + SEG * sl: 1024 * ci + SEG * (sl + 1)]
                dkey = f"kring{sl}"
            b.op("vector", lambda E, pB=pB, dst=dst: E.tensor_copy(dst, pB[:]),
                 reads=[pBk], writes=[dkey])

        if upto < 6:
            continue
        for h in range(16):
            b.op("vector", lambda E, h=h: E.reduce_max(duo[:, h:h + 1],
                                                       pb[5][:, 4 * h:4 * h + 4], axis=AXX),
                 reads=["pb5"], writes=["duo_q"])
            b.op("vector", lambda E, h=h: E.reduce_max(duo[:, 16 + h:17 + h],
                                                       pb[5][:, 64 + 4 * h:64 + 4 * h + 4], axis=AXX),
                 reads=["pb5"], writes=["duo_k"])
        b.op("vector", lambda E: E.tensor_tensor(kmaxr[:], kmaxr[:], duo[:, 16:32], op=ALU.max),
             reads=["duo_k", "kmaxr"], writes=["kmaxr"])
        b.op("vector", lambda E: E.tensor_copy(duo[:, 16:32], kmaxr[:]),
             reads=["kmaxr"], writes=["duo_k"])
        b.op("tensor", lambda E: E.transpose(pb[5][0:32, 0:128], duo[:], identf[:]),
             reads=["duo_q", "duo_k", "identf"], writes=["pb5"])
        b.op("vector", lambda E: E.reduce_max(lnst[0:32, 12:13], pb[5][0:32, 0:128], axis=AXX),
             reads=["pb5"], writes=["nmax32"])
        b.op("scalar", lambda E: E.activation(lnst[0:32, 13:14], lnst[0:32, 12:13], AF.Sqrt),
             reads=["nmax32"], writes=["nsqrt32"])
        b.op("tensor", lambda E: E.transpose(pb[5][0:1, 0:32], lnst[0:32, 13:14],
                                             identf[0:32, 0:32]),
             reads=["nsqrt32", "identf"], writes=["pb5"])
        b.op("vector", lambda E: E.tensor_copy(crow[0:1, 0:32], pb[5][0:1, 0:32]),
             reads=["pb5"], writes=["crow32"])
        b.op("vector", lambda E: E.tensor_tensor(crow[0:1, 32:48], crow[0:1, 0:16],
                                                 crow[0:1, 16:32], op=ALU.mult),
             reads=["crow32"], writes=["crow_negC"])
        b.op("vector", lambda E: E.tensor_scalar(crow[0:1, 32:48], crow[0:1, 32:48],
                                                 -0.125, None, ALU.mult),
             reads=["crow_negC"], writes=["crow_negC"])

        def mmbc(E):
            last = None
            for h in range(16):
                last = E.matmul(pb[5][:, h:h + 1], onesf[0:1, :], crow[0:1, 32 + h:33 + h],
                                start=True, stop=True)
            return last
        b.op("tensor", mmbc, reads=["crow_negC", "onesf"], writes=["pb5"])
        b.op("vector", lambda E: E.tensor_copy(negC[:], pb[5][:, 0:16]),
             reads=["pb5"], writes=["negC"])

        if upto < 7:
            continue
        for tt in range(4):
            for chf in range(2):
                pA = pb[0] if (tt * 2 + chf) % 2 == 0 else pb[6]
                pAk = "pb0" if (tt * 2 + chf) % 2 == 0 else "pb6"

                def mmv(E, pA=pA, tt=tt, chf=chf):
                    last = None
                    for ci in range(8):
                        last = E.matmul(pA[:], hfm[:, SEG * ci + 128 * tt: SEG * ci + 128 * (tt + 1)],
                                        WT[:, 3072 * ci + 2048 + 512 * chf: 3072 * ci + 2048 + 512 * (chf + 1)],
                                        start=(ci == 0), stop=(ci == 7))
                    return last
                b.op("tensor", mmv, reads=["hfm", "WT"], writes=[pAk])
                slot = (4 * s + tt) % 8
                dst = bass.AP(vring, 1040 * slot + 65 * 8 * chf,
                              [[8 * 1040, 128], [65, 8], [1, 64]])
                src3 = bass.AP(pA, 0, [[512, 128], [64, 8], [1, 64]])
                b.op("vector", lambda E, src3=src3, dst=dst: E.tensor_copy(dst, src3),
                     reads=[pAk], writes=[f"vring{slot}"])

        if upto < 8:
            continue
        for h in range(16):
            ch, hh = h // 2, h % 2
            alist = [a for a in range(-3, 4) if q0 + 128 * a >= 0]
            for ia, a in enumerate(alist):
                ktg = q0 // 128 + a
                slot = ktg % 8
                ksl = (ktg // 4) % 2
                kj = ktg % 4
                pA = pb[0] if ia % 2 == 0 else pb[6]
                pAk = "pb0" if ia % 2 == 0 else "pb6"
                kcol = 1024 * ch + 512 * ksl + 128 * kj
                b.op("tensor", lambda E, pA=pA, kcol=kcol, hh=hh, ch=ch:
                     E.matmul(pA[:], kring[64 * hh:64 * (hh + 1), kcol:kcol + 128],
                              qfm[64 * hh:64 * (hh + 1), SEG * ch:SEG * (ch + 1)],
                              start=True, stop=True),
                     reads=[f"kring{ksl}", "qfm"], writes=[pAk])
                pbi = ia % 2
                b.op("scalar", lambda E, pA=pA, pbi=pbi, h=h: E.activation(
                    Pbuf[:, SEG * pbi:SEG * (pbi + 1)], pA[:], AF.Exp,
                    bias=negC[:, h:h + 1], scale=0.125),
                    reads=[pAk, "negC"], writes=[f"praw{pbi}"])
                b.op("gpsimd", lambda E, pbi=pbi, a=a: E.tensor_tensor(
                    Pm[:, SEG * pbi:SEG * (pbi + 1)], Pbuf[:, SEG * pbi:SEG * (pbi + 1)],
                    maskL0[:, SEG * (a + 3):SEG * (a + 4)], op=ALU.mult),
                    reads=[f"praw{pbi}", "maskL0"], writes=[f"pm{pbi}"])
                b.op("tensor", lambda E, slot=slot, h=h, pbi=pbi, st=(ia == 0), sp=(ia == len(alist) - 1):
                     E.matmul(pb[2][0:65, :],
                              vring[:, 1040 * slot + 65 * h: 1040 * slot + 65 * (h + 1)],
                              Pm[:, SEG * pbi:SEG * (pbi + 1)], start=st, stop=sp),
                     reads=[f"pm{pbi}", f"vring{slot}", "vr_ones"], writes=["pb2"])
            b.op("vector", lambda E: E.reciprocal(rls[64:65, :], pb[2][64:65, :]),
                 reads=["pb2"], writes=["rlrow"])
            b.op("tensor", lambda E: E.matmul(pb[3][0:64, :], onesf[64:65, 0:64], rls[64:65, :],
                                              start=True, stop=True),
                 reads=["rlrow", "onesf"], writes=["pb3"])
            b.op("vector", lambda E: E.tensor_copy(rls[0:64, :], pb[3][0:64, :]),
                 reads=["pb3"], writes=["rl64"])
            b.op("vector", lambda E, ch=ch, hh=hh: E.tensor_tensor(
                AVN[64 * hh:64 * (hh + 1), SEG * ch:SEG * (ch + 1)],
                pb[2][0:64, :], rls[0:64, :], op=ALU.mult),
                reads=["pb2", "rl64"], writes=["AVN"])

        if upto < 9:
            continue
        mm_fm_tok(AVN, "AVN", WoT, "WoT", 1024, xseg, "xseg", x1seg, "x1seg")

        if upto < 10:
            continue
        layernorm(x1seg, "x1seg", htok, "htok")
        transpose_h()
        mm_fm_tok(hfm, "hfm", wtsT, "wtsT", 1024, x1seg, "x1seg", xseg, "xseg")
        for tt in range(4):
            dma("sync", x2out[q0 + 128 * tt:q0 + 128 * (tt + 1), :],
                xseg[:, 1024 * tt:1024 * (tt + 1)], ["xseg"], ["x2out"])

    b.barrier_wait_all("gpsimd")
    return b




def host_consts_l1():
    c = {}
    mp = np.zeros((128, 6 * 512), np.float32)
    mf = np.zeros((128, 6 * 512), np.float32)
    p = np.arange(128)[:, None]
    j = np.arange(512)[None, :]
    for ai, arel in enumerate(range(-1, 5)):
        d = (128 * arel + p) - j
        mp[:, ai * 512:(ai + 1) * 512] = (d <= -4).astype(np.float32)
        mf[:, ai * 512:(ai + 1) * 512] = (d >= 4).astype(np.float32)
    c["Mpast"] = mp.astype(bf)
    c["Mfut"] = mf.astype(bf)
    sh = np.zeros((128, 14 * 128), np.float32)
    for di, dlt in enumerate(range(-3, 4)):
        A = np.zeros((128, 128), np.float32)
        Bb = np.zeros((128, 128), np.float32)
        for m in range(128):
            k = m + dlt
            if 0 <= k < 128:
                A[k, m] = 1.0
            elif k < 0:
                Bb[k + 128, m] = 1.0
            else:
                Bb[k - 128, m] = 1.0
        sh[:, di * 256:di * 256 + 128] = A
        sh[:, di * 256 + 128:di * 256 + 256] = Bb
    c["Shm"] = sh.astype(bf)
    return c


def host_inputs_l1(core_half, table_k, table_v):
    c = {}
    tkx = np.zeros((128, 10), np.float32)
    tkx[0:64, 0:9] = table_k.T
    tkx[0:64, 9] = table_k[8 if core_half == 0 else 0]
    tkx[64:128] = tkx[0:64]
    c["tkX"] = tkx
    tvx = np.zeros((128, 260), np.float32)
    tvx[64, 0:64] = table_v[0]
    tvx[64, 65:129] = table_v[8]
    tvx[64, 130:194] = table_v[8 if core_half == 0 else 0]
    tvx[0:7, 195:259] = table_v[1:8]
    tvx[0:7, 259] = 1.0
    c["tvX"] = tvx
    mw = np.ones((128, 512), np.float32)
    if core_half == 1:
        p = np.arange(128)[:, None]
        j = np.arange(512)[None, :]
        d = (p + 31 * 128) - 4096 - j
        mw[(d >= -3) & (d <= 3)] = 0.0
    c["Mwrap"] = mw.astype(bf)
    qbase = core_half * 2048
    for name, t in (("btm0", 0), ("btm15", 15)):
        m = np.ones((128, 7), np.float32)
        qa = qbase + t * 128 + np.arange(128)[:, None]
        dl = np.arange(-3, 4)[None, :]
        m[(qa + dl < 0) | (qa + dl >= 4096)] = 0.0
        c[name] = m.astype(bf)
    return c


def build_l1a(nc, stack, nch=4):
    b = Bld(nc)
    x2r = nc.dram_tensor("x2r", [T_, D_], F32, kind="ExternalInput").ap()
    Wqd = nc.dram_tensor("Wq", [D_, D_], F32, kind="ExternalInput").ap()
    Wkd = nc.dram_tensor("Wk", [D_, D_], F32, kind="ExternalInput").ap()
    Wvd = nc.dram_tensor("Wv", [D_, D_], F32, kind="ExternalInput").ap()
    Wod = nc.dram_tensor("Wo", [D_, D_], F32, kind="ExternalInput").ap()
    brows_d = nc.dram_tensor("brows", [4, D_], F32, kind="ExternalInput").ap()
    tkXd = nc.dram_tensor("tkX", [128, 10], F32, kind="ExternalInput").ap()
    tvXd = nc.dram_tensor("tvX", [128, 260], F32, kind="ExternalInput").ap()
    Mpd = nc.dram_tensor("Mpast", [128, 6 * 512], BF16, kind="ExternalInput").ap()
    Mfd = nc.dram_tensor("Mfut", [128, 6 * 512], BF16, kind="ExternalInput").ap()
    Mwd = nc.dram_tensor("Mwrap", [128, 512], BF16, kind="ExternalInput").ap()
    btm0d = nc.dram_tensor("btm0", [128, 7], BF16, kind="ExternalInput").ap()
    btm15d = nc.dram_tensor("btm15", [128, 7], BF16, kind="ExternalInput").ap()
    Shd = nc.dram_tensor("Shm", [128, 14 * 128], BF16, kind="ExternalInput").ap()
    identbd = nc.dram_tensor("ident_b", [128, 128], BF16, kind="ExternalInput").ap()
    identfd = nc.dram_tensor("ident_f", [128, 128], F32, kind="ExternalInput").ap()
    x3out = nc.dram_tensor("x3", [2048, D_], F32, kind="ExternalOutput").ap()
    h2fmD = nc.dram_tensor("h2fmD", [128, 8 * T_], BF16).ap()
    qfmD = nc.dram_tensor("qfmD", [128, 8 * 2048], BF16).ap()

    kfm = sb(stack, nc, "kfm", [128, 8 * 4104], BF16)
    vxt = sb(stack, nc, "vxt", [128, 32 * 1040], BF16)
    Wos = sb(stack, nc, "Wos", [128, 8 * 1024], BF16)
    Mpast = sb(stack, nc, "Mpasts", [128, 6 * 512], BF16)
    Mfut = sb(stack, nc, "Mfuts", [128, 6 * 512], BF16)
    Mwrap = sb(stack, nc, "Mwraps", [128, 512], BF16)
    Shm = sb(stack, nc, "Shms", [128, 14 * 128], BF16)
    btm = sb(stack, nc, "btms", [128, 16], BF16)
    tkX = sb(stack, nc, "tkXs", [128, 10], BF16)
    tvX = sb(stack, nc, "tvXs", [128, 260], F32)
    onesb = sb(stack, nc, "onesb", [128, 1], BF16)
    identb = sb(stack, nc, "identbs", [128, 128], BF16)
    identf = sb(stack, nc, "identfs", [128, 128], F32)
    onesf = sb(stack, nc, "onesf", [128, 128], F32)
    negC = sb(stack, nc, "negC", [128, 1], F32)
    qmaxr = sb(stack, nc, "qmaxr", [128, 16], F32)
    kmaxr = sb(stack, nc, "kmaxr", [128, 16], F32)
    duo = sb(stack, nc, "duo", [128, 32], F32)
    crow = sb(stack, nc, "crow", [1, 64], F32)
    bbc = sb(stack, nc, "bbc", [128, 1024], F32)
    bfmj = sb(stack, nc, "bfmj", [128, 16], F32)
    lnst = sb(stack, nc, "lnst", [128, 16], F32)
    Pbuf = sb(stack, nc, "Pbuf", [128, 2 * SEG], BF16)
    Pm = sb(stack, nc, "Pmbuf", [128, 2 * SEG], BF16)
    rls = sb(stack, nc, "rls", [128, 512], F32)

    pb = [ps(stack, nc, f"pb{i}", [128, 512], F32) for i in range(8)]
    pb4b = pb[4].bitcast(BF16)
    pb5b = pb[5].bitcast(BF16)
    tsl = [0]

    def dma(eng, out, in_, reads, writes):
        b.op(eng, lambda E: E.dma_start(out=out, in_=in_), reads=reads, writes=writes, dma=True)

    def transpose_b(src_ap, rkeys, dst_ap, wkeys):
        sl_ = tsl[0]
        tsl[0] ^= 1
        view = pb4b if sl_ == 0 else pb5b
        pk = "pb4" if sl_ == 0 else "pb5"
        b.op("tensor", lambda E, view=view: E.transpose(view[:, 0:128], src_ap, identb[:]),
             reads=list(rkeys) + ["identb"], writes=[pk])
        b.op("vector", lambda E, view=view: E.tensor_copy(dst_ap, view[:, 0:128]),
             reads=[pk], writes=list(wkeys))

    dma("sync", Mpast[:], Mpd, [], ["Mpast"])
    dma("sync", Mfut[:], Mfd, [], ["Mfut"])
    dma("sync", Mwrap[:], Mwd, [], ["Mwrap"])
    dma("sync", Shm[:], Shd, [], ["Shm"])
    dma("sync", btm[:, 0:7], btm0d, [], ["btm"])
    dma("sync", btm[:, 8:15], btm15d, [], ["btm"])
    dma("sync", identb[:], identbd, [], ["identb"])
    dma("sync", identf[:], identfd, [], ["identf"])
    dma("gpsimd", tkX[:], tkXd, [], ["tkX"])
    dma("sync", tvX[:], tvXd, [], ["tvX"])
    b.op("vector", lambda E: E.memset(onesf[:], 1.0), writes=["onesf"])
    b.op("vector", lambda E: E.memset(onesb[:], 1.0), writes=["onesb"])
    b.op("vector", lambda E: E.memset(qmaxr[:], 0.0), writes=["qmaxr"])
    b.op("vector", lambda E: E.memset(kmaxr[:], 0.0), writes=["kmaxr"])
    b.op("vector", lambda E: E.memset(
        bass.AP(vxt, 64, [[32 * 1040, 128], [1040, 32], [65, 16], [1, 1]]), 1.0),
        writes=["vx_ones"])

    pscope = ExitStack()
    xseg = sb(pscope, nc, "xsegB", [128, 2 * 1024], F32)
    hseg = sb(pscope, nc, "hsegB", [128, 2 * 1024], BF16)
    hfm = sb(pscope, nc, "hfmB", [128, 8 * 512], BF16)
    brow = sb(pscope, nc, "browB", [1, 1024], F32)
    bvb = sb(pscope, nc, "bvbB", [128, 1024], F32)
    sqb = sb(pscope, nc, "sqbB", [128, SEG], F32)
    for bi, dco in ((0, 0), (1, 8)):
        dma("sync", brow[:], brows_d[bi:bi + 1, :], ["brow"], ["brow"])

        def mmb(E):
            last = None
            for ci in range(8):
                last = E.matmul(pb[5][:, ci:ci + 1],
                                brow[0:1, 128 * ci:128 * (ci + 1)],
                                onesf[0:1, 0:1], start=True, stop=True)
            return last
        b.op("tensor", mmb, reads=["brow", "onesf"], writes=["pb5"])
        b.op("vector", lambda E, dco=dco: E.tensor_copy(bfmj[:, dco:dco + 8], pb[5][:, 0:8]),
             reads=["pb5"], writes=["bfmj"])
    for bi, tgt, tkk in ((2, bvb, "bvb"), (3, bbc, "bbc")):
        dma("sync", brow[:], brows_d[bi:bi + 1, :], ["brow"], ["brow"])
        for half in range(2):
            b.op("tensor", lambda E, half=half: E.matmul(
                pb[5][:], onesf[0:1, :], brow[0:1, 512 * half:512 * (half + 1)],
                start=True, stop=True), reads=["brow", "onesf"], writes=["pb5"])
            b.op("vector", lambda E, tgt=tgt, half=half, tkk=tkk: E.tensor_copy(
                tgt[:, 512 * half:512 * (half + 1)], pb[5][:]),
                reads=["pb5"], writes=[tkk])

    def layernorm_tile(xa, srckey, dst, dstkey):
        b.op("vector", lambda E: E.reduce_sum(lnst[:, 0:1], xa, axis=AXX, negate=True),
             reads=[srckey], writes=["ln_negsum"])
        b.op("vector", lambda E: E.tensor_scalar(lnst[:, 1:2], lnst[:, 0:1],
                                                 1.0 / 1024.0, None, ALU.mult),
             reads=["ln_negsum"], writes=["ln_negmean"])
        b.op("scalar", lambda E: E.activation(pb[3][:], xa[:, 0:512], AF.Square,
                                              accum_out=lnst[:, 2:3]),
             reads=[srckey, "pb3"], writes=["pb3", "ln_sq0"])
        b.op("scalar", lambda E: E.activation(pb[3][:], xa[:, 512:1024], AF.Square,
                                              accum_out=lnst[:, 3:4]),
             reads=[srckey, "pb3"], writes=["pb3", "ln_sq1"])
        b.op("vector", lambda E: E.tensor_tensor(lnst[:, 2:3], lnst[:, 2:3],
                                                 lnst[:, 3:4], op=ALU.add),
             reads=["ln_sq0", "ln_sq1"], writes=["ln_sq0"])
        b.op("vector", lambda E: E.tensor_scalar(lnst[:, 4:5], lnst[:, 1:2],
                                                 lnst[:, 1:2], 1e-5, ALU.mult, ALU.subtract),
             reads=["ln_negmean"], writes=["ln_nm2"])
        b.op("vector", lambda E: E.scalar_tensor_tensor(lnst[:, 5:6], lnst[:, 2:3],
                                                        1.0 / 1024.0, lnst[:, 4:5],
                                                        ALU.mult, ALU.subtract),
             reads=["ln_sq0", "ln_nm2"], writes=["ln_var"])
        b.op("scalar", lambda E: E.activation(lnst[:, 6:7], lnst[:, 5:6], AF.Sqrt),
             reads=["ln_var"], writes=["ln_std"])
        b.op("vector", lambda E: E.reciprocal(lnst[:, 7:8], lnst[:, 6:7]),
             reads=["ln_std"], writes=["ln_rstd"])
        b.op("vector", lambda E: E.tensor_scalar(dst, xa, lnst[:, 1:2], lnst[:, 7:8],
                                                 ALU.add, ALU.mult),
             reads=[srckey, "ln_negmean", "ln_rstd"], writes=[dstkey])

    def load_w(wd):
        for ci in range(8):
            dma("gpsimd", Wos[:, 1024 * ci:1024 * (ci + 1)],
                wd[128 * ci:128 * (ci + 1), :], [], ["Wos"])

    def norm_cols(srckey, src_ap_fn, maxbuf, maxkey, ci):
        b.op("scalar", lambda E: E.activation(sqb[:], src_ap_fn(), AF.Square),
             reads=[srckey], writes=["sqb"])

        def mmn(E):
            last = None
            for hh in range(2):
                for qs in range(4):
                    last = E.matmul(pb[5][:, 8 * hh + qs: 8 * hh + qs + 1],
                                    sqb[64 * hh:64 * (hh + 1), 128 * qs:128 * (qs + 1)],
                                    onesf[64 * hh:64 * (hh + 1), 0:1],
                                    start=True, stop=True)
            return last
        b.op("tensor", mmn, reads=["sqb", "onesf"], writes=["pb5"])
        for hh in range(2):
            b.op("vector", lambda E, hh=hh: E.reduce_max(
                lnst[:, 8:9], pb[5][:, 8 * hh:8 * hh + 4], axis=AXX),
                reads=["pb5"], writes=["n_tmp"])
            b.op("vector", lambda E, ci=ci, hh=hh: E.tensor_tensor(
                maxbuf[:, 2 * ci + hh:2 * ci + hh + 1],
                maxbuf[:, 2 * ci + hh:2 * ci + hh + 1], lnst[:, 8:9], op=ALU.max),
                reads=["n_tmp", maxkey], writes=[maxkey])

    load_w(Wkd)
    for s in range(NSEG):
        q0 = s * SEG
        for tt in range(4):
            sl_t = tt % 2
            dma("sync", xseg[:, 1024 * sl_t:1024 * (sl_t + 1)],
                x2r[q0 + 128 * tt: q0 + 128 * (tt + 1), :], [], [f"xseg{sl_t}"])
            layernorm_tile(xseg[:, 1024 * sl_t:1024 * (sl_t + 1)], f"xseg{sl_t}",
                           hseg[:, 1024 * sl_t:1024 * (sl_t + 1)], f"hseg{sl_t}")
            for ci in range(8):
                transpose_b(hseg[:, 1024 * sl_t + 128 * ci: 1024 * sl_t + 128 * (ci + 1)],
                            [f"hseg{sl_t}"],
                            hfm[:, SEG * ci + 128 * tt: SEG * ci + 128 * (tt + 1)], ["hfm"])
        for ci in range(8):
            dma("gpsimd", h2fmD[:, T_ * ci + q0: T_ * ci + q0 + SEG],
                hfm[:, SEG * ci:SEG * (ci + 1)], ["hfm"], ["h2fmD"])
        for ci in range(8):
            pA = pb[0] if ci % 2 == 0 else pb[6]
            pAk = "pb0" if ci % 2 == 0 else "pb6"

            def mmk(E, pA=pA, ci=ci):
                last = None
                for kk in range(8):
                    last = E.matmul(pA[:], Wos[:, 1024 * kk + 128 * ci: 1024 * kk + 128 * (ci + 1)],
                                    hfm[:, SEG * kk: SEG * (kk + 1)],
                                    start=(kk == 0), stop=(kk == 7))
                return last
            b.op("tensor", mmk, reads=["Wos", "hfm"], writes=[pAk])
            kdst = kfm[:, 4104 * ci + 4 + q0: 4104 * ci + 4 + q0 + SEG]
            b.op("scalar", lambda E, pA=pA, ci=ci, kdst=kdst: E.activation(
                kdst, pA[:], AF.Identity, bias=bfmj[:, 8 + ci:9 + ci], scale=1.0),
                reads=[pAk, "bfmj"], writes=["kfm"])
            norm_cols("kfm", lambda ci=ci, q0=q0: kfm[:, 4104 * ci + 4 + q0: 4104 * ci + 4 + q0 + SEG],
                      kmaxr, "kmaxr", ci)
    for ci in range(8):
        b.op("vector", lambda E, ci=ci: E.tensor_copy(
            kfm[:, 4104 * ci: 4104 * ci + 4],
            kfm[:, 4104 * ci + 4 + 4092: 4104 * ci + 4 + 4096]),
            reads=["kfm"], writes=["kfm"])
        b.op("vector", lambda E, ci=ci: E.tensor_copy(
            kfm[:, 4104 * ci + 4 + 4096: 4104 * ci + 4 + 4100],
            kfm[:, 4104 * ci + 4: 4104 * ci + 8]),
            reads=["kfm"], writes=["kfm"])

    load_w(Wvd)
    for s in range(NSEG):
        q0 = s * SEG
        for ci in range(8):
            dma("sync", hfm[:, SEG * ci:SEG * (ci + 1)],
                h2fmD[:, T_ * ci + q0: T_ * ci + q0 + SEG], ["h2fmD"], ["hfm"])
        for tt in range(4):
            for chf in range(2):
                pA = pb[0] if (tt * 2 + chf) % 2 == 0 else pb[6]
                pAk = "pb0" if (tt * 2 + chf) % 2 == 0 else "pb6"

                def mmv(E, pA=pA, tt=tt, chf=chf):
                    last = None
                    for kk in range(8):
                        last = E.matmul(pA[:], hfm[:, SEG * kk + 128 * tt: SEG * kk + 128 * (tt + 1)],
                                        Wos[:, 1024 * kk + 512 * chf: 1024 * kk + 512 * (chf + 1)],
                                        start=(kk == 0), stop=(kk == 7))
                    return last
                b.op("tensor", mmv, reads=["hfm", "Wos"], writes=[pAk])
                slot = 4 * s + tt
                b.op("vector", lambda E, pA=pA, chf=chf: E.tensor_tensor(
                    rls[:], pA[:], bvb[:, 512 * chf:512 * (chf + 1)], op=ALU.add),
                    reads=[pAk, "bvb"], writes=["rls"])
                dst = bass.AP(vxt, 1040 * slot + 65 * 8 * chf,
                              [[32 * 1040, 128], [65, 8], [1, 64]])
                src3 = bass.AP(rls, 0, [[512, 128], [64, 8], [1, 64]])
                b.op("vector", lambda E, src3=src3, dst=dst: E.tensor_copy(dst, src3),
                     reads=["rls"], writes=[f"vxt{slot}"])

    load_w(Wqd)
    for s in range(4):
        q0 = s * SEG
        for ci in range(8):
            dma("sync", hfm[:, SEG * ci:SEG * (ci + 1)],
                h2fmD[:, T_ * ci + q0: T_ * ci + q0 + SEG], ["h2fmD"], ["hfm"])
        for ci in range(8):
            pA = pb[0] if ci % 2 == 0 else pb[6]
            pAk = "pb0" if ci % 2 == 0 else "pb6"

            def mmq(E, pA=pA, ci=ci):
                last = None
                for kk in range(8):
                    last = E.matmul(pA[:], Wos[:, 1024 * kk + 128 * ci: 1024 * kk + 128 * (ci + 1)],
                                    hfm[:, SEG * kk: SEG * (kk + 1)],
                                    start=(kk == 0), stop=(kk == 7))
                return last
            b.op("tensor", mmq, reads=["Wos", "hfm"], writes=[pAk])
            b.op("scalar", lambda E, pA=pA, ci=ci: E.activation(
                rls[:], pA[:], AF.Identity, bias=bfmj[:, ci:ci + 1], scale=1.0),
                reads=[pAk, "bfmj"], writes=["rls"])
            dma("gpsimd", qfmD[:, 2048 * ci + q0: 2048 * ci + q0 + SEG], rls[:],
                ["rls"], ["qfmD"])
            norm_cols("rls", lambda: rls[:], qmaxr, "qmaxr", ci)

    b.op("vector", lambda E: E.tensor_copy(duo[:, 0:16], qmaxr[:]),
         reads=["qmaxr"], writes=["duo"])
    b.op("vector", lambda E: E.tensor_copy(duo[:, 16:32], kmaxr[:]),
         reads=["kmaxr"], writes=["duo"])
    b.op("tensor", lambda E: E.transpose(pb[5][0:32, 0:128], duo[:], identf[:]),
         reads=["duo", "identf"], writes=["pb5"])
    b.op("vector", lambda E: E.reduce_max(lnst[0:32, 12:13], pb[5][0:32, 0:128], axis=AXX),
         reads=["pb5"], writes=["nmax32"])
    b.op("scalar", lambda E: E.activation(lnst[0:32, 13:14], lnst[0:32, 12:13], AF.Sqrt),
         reads=["nmax32"], writes=["nsqrt32"])
    b.op("tensor", lambda E: E.transpose(pb[5][0:1, 0:32], lnst[0:32, 13:14],
                                         identf[0:32, 0:32]),
         reads=["nsqrt32", "identf"], writes=["pb5"])
    b.op("vector", lambda E: E.tensor_copy(crow[0:1, 0:32], pb[5][0:1, 0:32]),
         reads=["pb5"], writes=["crow32"])
    b.op("vector", lambda E: E.tensor_tensor(crow[0:1, 32:48], crow[0:1, 0:16],
                                             crow[0:1, 16:32], op=ALU.mult),
         reads=["crow32"], writes=["crow_qk"])
    b.op("vector", lambda E: E.reduce_max(crow[0:1, 48:49], crow[0:1, 32:48], axis=AXX),
         reads=["crow_qk"], writes=["crow_gc"])
    b.op("vector", lambda E: E.reduce_max(crow[0:1, 49:50], crow[0:1, 0:16], axis=AXX),
         reads=["crow32"], writes=["crow_qg"])
    b.op("scalar", lambda E: E.activation(sqb[:, 0:10], tkX[:, 0:10], AF.Square),
         reads=["tkX"], writes=["sqb"])
    b.op("tensor", lambda E: E.matmul(pb[5][0:10, 0:1], sqb[0:64, 0:10], onesf[0:64, 0:1],
                                      start=True, stop=True),
         reads=["sqb", "onesf"], writes=["pb5"])
    b.op("vector", lambda E: E.tensor_copy(lnst[0:10, 14:15], pb[5][0:10, 0:1]),
         reads=["pb5"], writes=["tkn2"])
    b.op("scalar", lambda E: E.activation(lnst[0:10, 15:16], lnst[0:10, 14:15], AF.Sqrt),
         reads=["tkn2"], writes=["tknr"])
    b.op("tensor", lambda E: E.transpose(pb[5][0:1, 0:10], lnst[0:10, 15:16],
                                         identf[0:10, 0:10]),
         reads=["tknr", "identf"], writes=["pb5"])
    b.op("vector", lambda E: E.tensor_copy(crow[0:1, 50:60], pb[5][0:1, 0:10]),
         reads=["pb5"], writes=["crow_tk"])
    b.op("vector", lambda E: E.reduce_max(crow[0:1, 60:61], crow[0:1, 50:60], axis=AXX),
         reads=["crow_tk"], writes=["crow_tkg"])
    b.op("vector", lambda E: E.tensor_tensor(crow[0:1, 61:62], crow[0:1, 49:50],
                                             crow[0:1, 60:61], op=ALU.mult),
         reads=["crow_qg", "crow_tkg"], writes=["crow_sl"])
    b.op("vector", lambda E: E.tensor_tensor(crow[0:1, 62:63], crow[0:1, 48:49],
                                             crow[0:1, 61:62], op=ALU.add),
         reads=["crow_gc", "crow_sl"], writes=["crow_c"])
    b.op("vector", lambda E: E.tensor_scalar(crow[0:1, 62:63], crow[0:1, 62:63],
                                             -1.10 / 8.0, None, ALU.mult),
         reads=["crow_c"], writes=["crow_c"])
    b.op("tensor", lambda E: E.matmul(pb[5][:, 0:1], onesf[0:1, :], crow[0:1, 62:63],
                                      start=True, stop=True),
         reads=["crow_c", "onesf"], writes=["pb5"])
    b.op("vector", lambda E: E.tensor_copy(negC[:], pb[5][:, 0:1]),
         reads=["pb5"], writes=["negC"])

    pscope.close()
    for _eng in ENGS:
        b.barrier_wait_all(_eng)
    qfmc = sb(stack, nc, "qfmc", [128, 8 * 512], BF16)
    hadb = sb(stack, nc, "hadb", [128, 7 * 128], BF16)
    btok = sb(stack, nc, "btok", [128, 2 * 8], BF16)
    avbt = sb(stack, nc, "avbt", [128, 64], F32)
    avbf = sb(stack, nc, "avbf", [128, 512], F32)
    srows = sb(stack, nc, "srows", [7, 128], F32)
    avT = sb(stack, nc, "avT", [128, 512], F32)
    fsc = sb(stack, nc, "fsc", [128, 512], F32)
    frows = sb(stack, nc, "frows", [1, 512], F32)
    AVN = sb(stack, nc, "AVN", [128, 8 * 512], BF16)
    xres = sb(stack, nc, "xres", [128, 2 * 512], F32)
    x3s = sb(stack, nc, "x3s", [128, 512], F32)
    load_w(Wod)

    for ch in range(nch):
        q0 = ch * SEG
        cpar = 0
        qbase_c = 0
        for ci in range(8):
            dma("sync", qfmc[:, qbase_c + 512 * ci: qbase_c + 512 * (ci + 1)],
                qfmD[:, 2048 * ci + q0: 2048 * ci + q0 + SEG],
                ["qfmD"], ["qfmc"])

        for h in range(16):
            ci, hh = h // 2, h % 2

            def qsl(lo, width, rows64=True):
                base = qbase_c + 512 * ci
                if rows64:
                    return qfmc[64 * hh:64 * (hh + 1), base + lo: base + lo + width]
                return qfmc[:, base + lo: base + lo + width]

            b.op("vector", lambda E: E.memset(avbf[:], 0.0),
                 reads=[], writes=["avbf"])
            for qt in range(4):
                t = 4 * ch + qt
                tq0 = q0 + 128 * qt
                bpar = qt % 2
                if hh == 0:
                    for di, dlt in enumerate(range(-3, 4)):
                        qap = qsl(128 * qt, 128, rows64=False)
                        kap = kfm[:, 4104 * ci + 4 + tq0 + dlt:
                                  4104 * ci + 4 + tq0 + dlt + 128]
                        b.op("vector", lambda E, di=di, qap=qap, kap=kap:
                             E.tensor_tensor(hadb[:, 128 * di:128 * (di + 1)],
                                             qap, kap, op=ALU.mult),
                             reads=["qfmc", "kfm"], writes=[f"hadb{qt}"])

                qap_t = qsl(128 * qt, 128)

                def mmband(E, hh=hh, qap_t=qap_t):
                    for di in range(7):
                        E.matmul(pb[4][:, di:di + 1],
                                 hadb[64 * hh:64 * (hh + 1), 128 * di:128 * (di + 1)],
                                 onesb[64 * hh:64 * (hh + 1), 0:1],
                                 start=True, stop=True)
                    return E.matmul(pb[4][:, 8:15], qap_t,
                                    tkX[64 * hh:64 * (hh + 1), 1:8],
                                    start=True, stop=True)
                b.op("tensor", mmband, reads=[f"hadb{qt}", "onesb", "qfmc", "tkX"],
                     writes=["pb4"])
                b.op("vector", lambda E: E.tensor_copy(lnst[:, 0:7], pb[4][:, 8:15]),
                     reads=["pb4"], writes=["lnq7"])
                b.op("vector", lambda E: E.tensor_tensor(lnst[:, 8:15], pb[4][:, 0:7],
                                                         lnst[:, 0:7], op=ALU.add),
                     reads=["pb4", "lnq7"], writes=["lns7"])
                b.op("scalar", lambda E, bpar=bpar: E.activation(
                    btok[:, 8 * bpar:8 * bpar + 7], lnst[:, 8:15], AF.Exp,
                    bias=negC[:, 0:1], scale=0.125),
                    reads=["lns7", "negC"], writes=[f"btok{bpar}"])
                if t in (0, 15):
                    mo = 0 if t == 0 else 8
                    b.op("vector", lambda E, bpar=bpar, mo=mo: E.tensor_tensor(
                        btok[:, 8 * bpar:8 * bpar + 7], btok[:, 8 * bpar:8 * bpar + 7],
                        btm[:, mo:mo + 7], op=ALU.mult),
                        reads=[f"btok{bpar}", "btm"], writes=[f"btok{bpar}"])
                b.op("tensor", lambda E, bpar=bpar: E.transpose(
                    pb5b[0:7, 0:128], btok[:, 8 * bpar:8 * bpar + 7], identb[:]),
                    reads=[f"btok{bpar}", "identb"], writes=["pb5"])
                b.op("vector", lambda E: E.tensor_copy(srows[:], pb5b[0:7, 0:128]),
                     reads=["pb5"], writes=["srows"])
                b.op("tensor", lambda E, qt=qt: E.matmul(
                    pb[7][0:65, 128 * qt:128 * (qt + 1)], tvX[0:7, 195:260], srows[:],
                    start=True, stop=True),
                    reads=["srows", "tvX"], writes=["pb7"])
                def mmshift(E, t=t, h=h):
                    last = None
                    for di, dlt in enumerate(range(-3, 4)):
                        tB = (t + (1 if dlt > 0 else -1)) % 32
                        last = E.matmul(pb[4][:, 64 * di:64 * (di + 1)],
                                        Shm[:, 256 * di:256 * di + 128],
                                        vxt[:, 1040 * t + 65 * h: 1040 * t + 65 * h + 64],
                                        start=True, stop=(dlt == 0))
                        if dlt != 0:
                            last = E.matmul(pb[4][:, 64 * di:64 * (di + 1)],
                                            Shm[:, 256 * di + 128:256 * di + 256],
                                            vxt[:, 1040 * tB + 65 * h: 1040 * tB + 65 * h + 64],
                                            start=False, stop=True)
                    return last
                b.op("tensor", mmshift,
                     reads=["Shm", f"vxt{t}", f"vxt{(t + 1) % 32}", f"vxt{(t - 1) % 32}"],
                     writes=["pb4"])
                bview = bass.AP(btok, 8 * bpar, [[16, 128], [1, 7], [0, 64]])
                b.op("vector", lambda E, bview=bview: E.tensor_tensor(
                    bass.AP(rls, 0, [[512, 128], [64, 7], [1, 64]]),
                    bass.AP(pb[4], 0, [[512, 128], [64, 7], [1, 64]]), bview, op=ALU.mult),
                    reads=["pb4", f"btok{bpar}"], writes=["rls"])
                b.op("vector", lambda E: E.reduce_sum(
                    avbt[:], bass.AP(rls, 0, [[512, 128], [1, 64], [64, 7]]), axis=AXX),
                    reads=["rls"], writes=["avbt"])
                b.op("tensor", lambda E: E.transpose(pb[4][0:64, 0:128], avbt[:], identf[:]),
                     reads=["avbt", "identf"], writes=["pb4"])
                b.op("vector", lambda E, qt=qt: E.tensor_copy(
                    avbf[0:64, 128 * qt:128 * (qt + 1)], pb[4][0:64, 0:128]),
                    reads=["pb4"], writes=["avbf"])

            def classify(kt):
                k0 = 128 * kt
                if k0 + 127 <= q0 - 4:
                    return "P"
                if k0 >= q0 + 512 + 4:
                    return "F"
                return "S"
            listP = [kt for kt in range(32) if classify(kt) in ("P", "S")]
            listFlo = [kt for kt in range(32) if classify(kt) == "S"
                       or (classify(kt) == "F" and kt < 16)]
            listFhi = [kt for kt in range(32) if classify(kt) == "F" and kt >= 16]
            for kt in range(32):
                k0 = 128 * kt
                cls = classify(kt)
                spar = kt % 2
                pS = pb[0] if spar == 0 else pb[6]
                pSk = "pb0" if spar == 0 else "pb6"
                qap_c = qsl(0, 512)
                b.op("tensor", lambda E, pS=pS, ci=ci, hh=hh, k0=k0, qap_c=qap_c:
                     E.matmul(pS[:], kfm[64 * hh:64 * (hh + 1),
                                         4104 * ci + 4 + k0: 4104 * ci + 4 + k0 + 128],
                              qap_c, start=True, stop=True),
                     reads=["kfm", "qfmc"], writes=[pSk])
                b.op("scalar", lambda E, pS=pS, spar=spar: E.activation(
                    Pbuf[:, SEG * spar:SEG * (spar + 1)], pS[:], AF.Exp,
                    bias=negC[:, 0:1], scale=0.125),
                    reads=[pSk, "negC"], writes=[f"praw{spar}"])
                if cls == "P":
                    b.op("tensor", lambda E, kt=kt, h=h, spar=spar,
                         st=(kt == listP[0]), sp=(kt == listP[-1]):
                         E.matmul(pb[1][0:65, :],
                                  vxt[:, 1040 * kt + 65 * h: 1040 * kt + 65 * h + 65],
                                  Pbuf[:, SEG * spar:SEG * (spar + 1)], start=st, stop=sp),
                         reads=[f"praw{spar}", f"vxt{kt}", "vx_ones"], writes=["pb1"])
                elif cls == "F":
                    hi = kt >= 16
                    if kt == 31 and ch == 0:
                        b.op("gpsimd", lambda E, spar=spar: E.tensor_tensor(
                            Pm[:, SEG * spar:SEG * (spar + 1)],
                            Pbuf[:, SEG * spar:SEG * (spar + 1)], Mwrap[:], op=ALU.mult),
                            reads=[f"praw{spar}", "Mwrap"], writes=[f"pm{spar}"])
                        Pap, rkey = Pm[:, SEG * spar:SEG * (spar + 1)], f"pm{spar}"
                    else:
                        Pap, rkey = Pbuf[:, SEG * spar:SEG * (spar + 1)], f"praw{spar}"
                    tgt, tkey, lst = ((pb[3], "pb3", listFhi) if hi
                                      else (pb[2], "pb2", listFlo))
                    b.op("tensor", lambda E, kt=kt, h=h, Pap=Pap, tgt=tgt,
                         st=(kt == lst[0]), sp=(kt == lst[-1]):
                         E.matmul(tgt[0:65, :],
                                  vxt[:, 1040 * kt + 65 * h: 1040 * kt + 65 * h + 65],
                                  Pap, start=st, stop=sp),
                         reads=[rkey, f"vxt{kt}", "vx_ones"], writes=[tkey])
                else:
                    ai = (kt - 4 * ch) + 1
                    b.op("gpsimd", lambda E, spar=spar, ai=ai: E.tensor_tensor(
                        Pm[:, SEG * spar:SEG * (spar + 1)],
                        Pbuf[:, SEG * spar:SEG * (spar + 1)],
                        Mpast[:, 512 * ai:512 * (ai + 1)], op=ALU.mult),
                        reads=[f"praw{spar}", "Mpast"], writes=[f"pm{spar}"])
                    b.op("tensor", lambda E, kt=kt, h=h, spar=spar,
                         st=(kt == listP[0]), sp=(kt == listP[-1]):
                         E.matmul(pb[1][0:65, :],
                                  vxt[:, 1040 * kt + 65 * h: 1040 * kt + 65 * h + 65],
                                  Pm[:, SEG * spar:SEG * (spar + 1)], start=st, stop=sp),
                         reads=[f"pm{spar}", f"vxt{kt}", "vx_ones"], writes=["pb1"])
                    b.op("vector", lambda E, spar=spar, ai=ai: E.tensor_tensor(
                        Pm[:, SEG * spar:SEG * (spar + 1)],
                        Pbuf[:, SEG * spar:SEG * (spar + 1)],
                        Mfut[:, 512 * ai:512 * (ai + 1)], op=ALU.mult),
                        reads=[f"praw{spar}", "Mfut", f"pm{spar}"], writes=[f"pm{spar}"])
                    b.op("tensor", lambda E, kt=kt, h=h, spar=spar,
                         st=(kt == listFlo[0]), sp=(kt == listFlo[-1]):
                         E.matmul(pb[2][0:65, :],
                                  vxt[:, 1040 * kt + 65 * h: 1040 * kt + 65 * h + 65],
                                  Pm[:, SEG * spar:SEG * (spar + 1)], start=st, stop=sp),
                         reads=[f"pm{spar}", f"vxt{kt}", "vx_ones"], writes=["pb2"])

            qap_f = qsl(0, 512)
            combos = ((0, 0, "pb1", pb[1]), (1, 8, "pb2", pb[2]), (2, 9, "pb3", pb[3]))
            for fi, tcol, pk, pp in combos:
                b.op("tensor", lambda E, tcol=tcol, hh=hh, qap_f=qap_f: E.matmul(
                    pb[5][0:1, :], tkX[64 * hh:64 * (hh + 1), tcol:tcol + 1],
                    qap_f, start=True, stop=True),
                    reads=["qfmc", "tkX"], writes=["pb5"])
                b.op("scalar", lambda E: E.activation(
                    frows[0:1, :], pb[5][0:1, :], AF.Exp, bias=0.0, scale=0.125),
                    reads=["pb5"], writes=["frows"])
                b.op("tensor", lambda E: E.matmul(
                    pb[5][0:65, :], onesf[0:1, 0:65], frows[0:1, :],
                    start=True, stop=True),
                    reads=["frows", "onesf"], writes=["pb5"])
                b.op("vector", lambda E: E.tensor_copy(fsc[0:65, :], pb[5][0:65, :]),
                     reads=["pb5"], writes=["fsc"])
                if fi == 0:
                    b.op("vector", lambda E, pp=pp: E.tensor_tensor(
                        avT[0:65, :], pp[0:65, :], fsc[0:65, :], op=ALU.mult),
                        reads=[pk, "fsc"], writes=["avT"])
                    b.op("tensor", lambda E: E.matmul(
                        pb[4][0:65, :], tvX[64:65, 0:65], avT[64:65, :],
                        start=True, stop=False),
                        reads=["avT", "tvX"], writes=["pb4"])
                else:
                    b.op("vector", lambda E, pp=pp: E.tensor_tensor(
                        rls[0:65, :], pp[0:65, :], fsc[0:65, :], op=ALU.mult),
                        reads=[pk, "fsc"], writes=["rls"])
                    tco = 65 if fi == 1 else 130
                    b.op("tensor", lambda E, tco=tco, fi=fi: E.matmul(
                        pb[4][0:65, :], tvX[64:65, tco:tco + 65],
                        rls[64:65, :], start=False, stop=(fi == 2)),
                        reads=["rls", "tvX"], writes=["pb4"])
                    b.op("vector", lambda E: E.tensor_tensor(
                        avT[0:65, :], avT[0:65, :], rls[0:65, :], op=ALU.add),
                        reads=["avT", "rls"], writes=["avT"])
            b.op("vector", lambda E: E.tensor_tensor(
                avT[0:65, :], avT[0:65, :], pb[7][0:65, :], op=ALU.add),
                reads=["avT", "pb7"], writes=["avT"])
            b.op("vector", lambda E: E.tensor_tensor(
                avT[0:65, :], avT[0:65, :], pb[4][0:65, :], op=ALU.add),
                reads=["avT", "pb4"], writes=["avT"])
            b.op("vector", lambda E: E.tensor_tensor(
                avT[0:64, :], avT[0:64, :], avbf[0:64, :], op=ALU.add),
                reads=["avT", "avbf"], writes=["avT"])
            b.op("vector", lambda E: E.reciprocal(rls[64:65, :], avT[64:65, :]),
                 reads=["avT"], writes=["rls"])
            b.op("tensor", lambda E: E.matmul(pb[5][0:64, :], onesf[64:65, 0:64],
                                              rls[64:65, :], start=True, stop=True),
                 reads=["rls", "onesf"], writes=["pb5"])
            b.op("vector", lambda E: E.tensor_copy(fsc[0:64, :], pb[5][0:64, :]),
                 reads=["pb5"], writes=["fsc"])
            b.op("vector", lambda E, ci=ci, hh=hh: E.tensor_tensor(
                AVN[64 * hh:64 * (hh + 1), 512 * ci:512 * (ci + 1)],
                avT[0:64, :], fsc[0:64, :], op=ALU.mult),
                reads=["avT", "fsc"], writes=["AVN"])

        for tsub in range(4):
            for chf in range(2):
                par = (tsub * 2 + chf) % 2
                pA = pb[0] if par == 0 else pb[6]
                pAk = "pb0" if par == 0 else "pb6"
                dma("sync", xres[:, 512 * par:512 * (par + 1)],
                    x2r[q0 + 128 * tsub: q0 + 128 * (tsub + 1), 512 * chf:512 * (chf + 1)],
                    [f"xres{par}"], [f"xres{par}"])

                def mmo(E, pA=pA, tsub=tsub, chf=chf):
                    last = None
                    for kk in range(8):
                        last = E.matmul(pA[:],
                                        AVN[:, 512 * kk + 128 * tsub: 512 * kk + 128 * (tsub + 1)],
                                        Wos[:, 1024 * kk + 512 * chf: 1024 * kk + 512 * (chf + 1)],
                                        start=(kk == 0), stop=(kk == 7))
                    return last
                b.op("tensor", mmo, reads=["AVN", "Wos"], writes=[pAk])
                b.op("vector", lambda E, pA=pA, chf=chf: E.tensor_tensor(
                    x3s[:], pA[:],
                    bbc[:, 512 * chf:512 * (chf + 1)], op=ALU.add),
                    reads=[pAk, "bbc"], writes=["x3s"])
                b.op("vector", lambda E, par=par: E.tensor_tensor(
                    x3s[:], x3s[:],
                    xres[:, 512 * par:512 * (par + 1)], op=ALU.add),
                    reads=["x3s", f"xres{par}"], writes=["x3s"])
                dma("sync", x3out[q0 + 128 * tsub: q0 + 128 * (tsub + 1),
                                  512 * chf:512 * (chf + 1)],
                    x3s[:], ["x3s"], ["x3out"])

    b.barrier_wait_all("gpsimd")
    return b


def build_moe(nc, stack):
    b = Bld(nc)
    NT = 2048
    x3d = nc.dram_tensor("x3", [NT, D_], F32, kind="ExternalInput").ap()
    Wrd = nc.dram_tensor("Wr", [D_, 8], F32, kind="ExternalInput").ap()
    W1d = nc.dram_tensor("W1", [8 * D_, 4096], F32, kind="ExternalInput").ap()
    W2d = nc.dram_tensor("W2", [8 * 4096, D_], F32, kind="ExternalInput").ap()
    b1d = nc.dram_tensor("b1", [8, 4096], F32, kind="ExternalInput").ap()
    b2d = nc.dram_tensor("b2", [8, D_], F32, kind="ExternalInput").ap()
    identbd = nc.dram_tensor("ident_b", [128, 128], BF16, kind="ExternalInput").ap()
    x4out = nc.dram_tensor("x4", [NT, D_], F32, kind="ExternalOutput").ap()

    h3fm = sb(stack, nc, "h3fm", [128, 8 * NT], BF16)
    ytot = sb(stack, nc, "ytot", [128, 16 * 1024], BF16)
    G = sb(stack, nc, "G", [128, 128], F32)
    identb = sb(stack, nc, "identbs", [128, 128], BF16)
    onesf = sb(stack, nc, "onesf", [128, 128], F32)
    lnst = sb(stack, nc, "lnst", [128, 16], F32)
    b1f = sb(stack, nc, "b1f", [128, 32], F32)
    brow = sb(stack, nc, "brow", [1, 4096], BF16)
    onesb2 = sb(stack, nc, "onesb2", [1, 128], BF16)
    g1s = sb(stack, nc, "g1s", [128, 2 * 256], BF16)
    er = sb(stack, nc, "er", [128, 8], F32)

    pb = [ps(stack, nc, f"pb{i}", [128, 512], F32) for i in range(8)]
    pb4b = pb[4].bitcast(BF16)
    pb5b = pb[5].bitcast(BF16)
    tsl = [0]

    def dma(eng, out, in_, reads, writes):
        b.op(eng, lambda E: E.dma_start(out=out, in_=in_), reads=reads, writes=writes, dma=True)

    def transpose_b(src_ap, rkeys, dst_ap, wkeys):
        sl_ = tsl[0]
        tsl[0] ^= 1
        view = pb4b if sl_ == 0 else pb5b
        pk = "pb4" if sl_ == 0 else "pb5"
        b.op("tensor", lambda E, view=view: E.transpose(view[:, 0:128], src_ap, identb[:]),
             reads=list(rkeys) + ["identb"], writes=[pk])
        b.op("vector", lambda E, view=view: E.tensor_copy(dst_ap, view[:, 0:128]),
             reads=[pk], writes=list(wkeys))

    dma("sync", identb[:], identbd, [], ["identb"])
    b.op("vector", lambda E: E.memset(onesf[:], 1.0), writes=["onesf"])
    b.op("vector", lambda E: E.memset(onesb2[:], 1.0), writes=["onesb2"])
    b.op("vector", lambda E: E.memset(ytot[:], 0.0), writes=["ytot"])

    pscope = ExitStack()
    xseg = sb(pscope, nc, "xsegC", [128, 2 * 1024], F32)
    hseg = sb(pscope, nc, "hsegC", [128, 2 * 1024], BF16)
    Wrs = sb(pscope, nc, "WrsC", [128, 8 * 8], BF16)
    for ci in range(8):
        dma("gpsimd", Wrs[:, 8 * ci:8 * (ci + 1)], Wrd[128 * ci:128 * (ci + 1), :],
            [], ["Wrs"])

    def layernorm_tile(xa, srckey, dst, dstkey):
        b.op("vector", lambda E: E.reduce_sum(lnst[:, 0:1], xa, axis=AXX, negate=True),
             reads=[srckey], writes=["ln_negsum"])
        b.op("vector", lambda E: E.tensor_scalar(lnst[:, 1:2], lnst[:, 0:1],
                                                 1.0 / 1024.0, None, ALU.mult),
             reads=["ln_negsum"], writes=["ln_negmean"])
        b.op("scalar", lambda E: E.activation(pb[3][:], xa[:, 0:512], AF.Square,
                                              accum_out=lnst[:, 2:3]),
             reads=[srckey, "pb3"], writes=["pb3", "ln_sq0"])
        b.op("scalar", lambda E: E.activation(pb[3][:], xa[:, 512:1024], AF.Square,
                                              accum_out=lnst[:, 3:4]),
             reads=[srckey, "pb3"], writes=["pb3", "ln_sq1"])
        b.op("vector", lambda E: E.tensor_tensor(lnst[:, 2:3], lnst[:, 2:3],
                                                 lnst[:, 3:4], op=ALU.add),
             reads=["ln_sq0", "ln_sq1"], writes=["ln_sq0"])
        b.op("vector", lambda E: E.tensor_scalar(lnst[:, 4:5], lnst[:, 1:2],
                                                 lnst[:, 1:2], 1e-5, ALU.mult, ALU.subtract),
             reads=["ln_negmean"], writes=["ln_nm2"])
        b.op("vector", lambda E: E.scalar_tensor_tensor(lnst[:, 5:6], lnst[:, 2:3],
                                                        1.0 / 1024.0, lnst[:, 4:5],
                                                        ALU.mult, ALU.subtract),
             reads=["ln_sq0", "ln_nm2"], writes=["ln_var"])
        b.op("scalar", lambda E: E.activation(lnst[:, 6:7], lnst[:, 5:6], AF.Sqrt),
             reads=["ln_var"], writes=["ln_std"])
        b.op("vector", lambda E: E.reciprocal(lnst[:, 7:8], lnst[:, 6:7]),
             reads=["ln_std"], writes=["ln_rstd"])
        b.op("vector", lambda E: E.tensor_scalar(dst, xa, lnst[:, 1:2], lnst[:, 7:8],
                                                 ALU.add, ALU.mult),
             reads=[srckey, "ln_negmean", "ln_rstd"], writes=[dstkey])

    for t in range(16):
        sl_t = t % 2
        dma("sync", xseg[:, 1024 * sl_t:1024 * (sl_t + 1)],
            x3d[128 * t:128 * (t + 1), :], [], [f"xseg{sl_t}"])
        layernorm_tile(xseg[:, 1024 * sl_t:1024 * (sl_t + 1)], f"xseg{sl_t}",
                       hseg[:, 1024 * sl_t:1024 * (sl_t + 1)], f"hseg{sl_t}")
        for ci in range(8):
            transpose_b(hseg[:, 1024 * sl_t + 128 * ci: 1024 * sl_t + 128 * (ci + 1)],
                        [f"hseg{sl_t}"],
                        h3fm[:, NT * ci + 128 * t: NT * ci + 128 * (t + 1)], ["h3fm"])
        def mmr(E, t=t):
            last = None
            for ci in range(8):
                last = E.matmul(pb[2][:, 0:8],
                                h3fm[:, NT * ci + 128 * t: NT * ci + 128 * (t + 1)],
                                Wrs[:, 8 * ci:8 * (ci + 1)],
                                start=(ci == 0), stop=(ci == 7))
            return last
        b.op("tensor", mmr, reads=["h3fm", "Wrs"], writes=["pb2"])
        b.op("vector", lambda E: E.reduce_max(lnst[:, 8:9], pb[2][:, 0:8], axis=AXX,
                                              negate=True),
             reads=["pb2"], writes=["r_negm"])
        b.op("scalar", lambda E: E.activation(er[:], pb[2][:, 0:8], AF.Exp,
                                              bias=lnst[:, 8:9], scale=1.0,
                                              accum_out=lnst[:, 9:10]),
             reads=["pb2", "r_negm"], writes=["er", "r_sum"])
        b.op("vector", lambda E: E.reciprocal(lnst[:, 10:11], lnst[:, 9:10]),
             reads=["r_sum"], writes=["r_pmax"])
        b.op("vector", lambda E, t=t: E.tensor_scalar(
            G[:, 8 * t:8 * (t + 1)], pb[2][:, 0:8], lnst[:, 8:9], 0.0,
            ALU.add, ALU.is_equal),
            reads=["pb2", "r_negm"], writes=["G"])
        b.op("vector", lambda E, t=t: E.tensor_scalar(
            G[:, 8 * t:8 * (t + 1)], G[:, 8 * t:8 * (t + 1)], lnst[:, 10:11], None,
            ALU.mult),
            reads=["G", "r_pmax"], writes=["G"])
    pscope.close()
    for _eng in ENGS:
        b.barrier_wait_all(_eng)

    W1s = sb(stack, nc, "W1s", [128, 8 * 4096], BF16)
    W2s = sb(stack, nc, "W2s", [128, 32 * 1024], BF16)
    for e in range(8):
        for ci in range(8):
            dma("gpsimd", W1s[:, 4096 * ci:4096 * (ci + 1)],
                W1d[1024 * e + 128 * ci: 1024 * e + 128 * (ci + 1), :], [], ["W1s"])
        for fc in range(32):
            dma("gpsimd", W2s[:, 1024 * fc:1024 * (fc + 1)],
                W2d[4096 * e + 128 * fc: 4096 * e + 128 * (fc + 1), :], [], ["W2s"])
        dma("gpsimd", brow[:], b1d[e:e + 1, :], ["brow"], ["brow"])

        def mmb1(E):
            last = None
            for fc in range(32):
                last = E.matmul(pb[5][:, fc:fc + 1], brow[0:1, 128 * fc:128 * (fc + 1)],
                                onesb2[0:1, 0:1], start=True, stop=True)
            return last
        b.op("tensor", mmb1, reads=["brow", "onesf"], writes=["pb5"])
        b.op("vector", lambda E: E.tensor_copy(b1f[:], pb[5][:, 0:32]),
             reads=["pb5"], writes=["b1f"])
        dma("gpsimd", brow[0:1, 0:1024], b2d[e:e + 1, :], ["brow"], ["brow"])

        for tp in range(8):
            t0 = 256 * tp
            for fsub in range(32):
                gpar = fsub % 2
                pP = pb[4] if gpar == 0 else pb[5]
                pPk = "pb4" if gpar == 0 else "pb5"

                def mm1(E, pP=pP, fsub=fsub, t0=t0):
                    last = None
                    for kk in range(8):
                        last = E.matmul(
                            pP[:, 0:256],
                            W1s[:, 4096 * kk + 128 * fsub: 4096 * kk + 128 * (fsub + 1)],
                            h3fm[:, NT * kk + t0: NT * kk + t0 + 256],
                            start=(kk == 0), stop=(kk == 7))
                    return last
                b.op("tensor", mm1, reads=["W1s", "h3fm"], writes=[pPk])
                b.op("scalar", lambda E, pP=pP, gpar=gpar, fsub=fsub: E.activation(
                    g1s[:, 256 * gpar:256 * (gpar + 1)], pP[:, 0:256], AF.Gelu,
                    bias=b1f[:, fsub:fsub + 1], scale=1.0),
                    reads=[pPk, "b1f"], writes=[f"g1s{gpar}"])
                for ts2 in range(2):
                    for chf in range(2):
                        yb = 2 * ts2 + chf
                        b.op("tensor", lambda E, yb=yb, gpar=gpar, ts2=ts2, chf=chf, fsub=fsub:
                             E.matmul(pb[yb][:],
                                      g1s[:, 256 * gpar + 128 * ts2: 256 * gpar + 128 * (ts2 + 1)],
                                      W2s[:, 1024 * fsub + 512 * chf: 1024 * fsub + 512 * (chf + 1)],
                                      start=(fsub == 0), stop=False),
                             reads=[f"g1s{gpar}", "W2s"], writes=[f"pb{yb}"])
            for ts2 in range(2):
                for chf in range(2):
                    yb = 2 * ts2 + chf
                    b.op("tensor", lambda E, yb=yb, chf=chf: E.matmul(
                        pb[yb][:], onesb2[:], brow[0:1, 512 * chf:512 * (chf + 1)],
                        start=False, stop=True),
                        reads=["brow", "onesb2"], writes=[f"pb{yb}"])
            for ts2 in range(2):
                for chf in range(2):
                    yb = 2 * ts2 + chf
                    t = 2 * tp + ts2
                    dst = ytot[:, 1024 * t + 512 * chf: 1024 * t + 512 * (chf + 1)]
                    b.op("vector", lambda E, yb=yb, t=t, e=e, dst=dst: E.scalar_tensor_tensor(
                        dst, pb[yb][:], G[:, 8 * t + e: 8 * t + e + 1], dst,
                        ALU.mult, ALU.add),
                        reads=[f"pb{yb}", "G", "ytot"], writes=["ytot"])

    with ExitStack() as fs:
        xf = sb(fs, nc, "xfC", [128, 1024], F32)
        for t in range(16):
            dma("sync", xf[:], x3d[128 * t:128 * (t + 1), :], ["xf"], ["xf"])
            b.op("vector", lambda E, t=t: E.tensor_tensor(
                xf[:], xf[:], ytot[:, 1024 * t:1024 * (t + 1)], op=ALU.add),
                reads=["xf", "ytot"], writes=["xf"])
            dma("sync", x4out[128 * t:128 * (t + 1), :], xf[:], ["xf"], ["x4out"])

    b.barrier_wait_all("gpsimd")
    return b


_BUILT = {}


def _get_built(name, fn):
    if name not in _BUILT:
        nc = bass.Bass("TRN2", target_bir_lowering=False)
        stack = ExitStack()
        bb = fn(nc, stack)
        bb.finalize(stack)
        _BUILT[name] = (nc, stack, bb)
    return _BUILT[name][0]


def kernel(**inputs):
    C = host_consts()
    C1 = host_consts_l1()
    f32 = np.float32

    def arr(k):
        return np.ascontiguousarray(np.asarray(inputs[k], dtype=f32))

    x0 = arr("L0_attn_query")
    perm = l0_inproj_perm()
    win_p = np.ascontiguousarray(arr("L0_attn_in_proj_w")[perm])
    tk = arr("L1_attn_table_k")
    tv = arr("L1_attn_table_v")
    Chalf = [host_inputs_l1(h, tk, tv) for h in range(2)]
    brows = np.ascontiguousarray(np.stack([arr("L1_attn_bq"), arr("L1_attn_bk"),
                                           arr("L1_attn_bv"), arr("L1_attn_bo")]))

    ncA = _get_built("A", build_l0)
    mapsA = []
    for c in range(NCORE):
        mapsA.append({
            "x0": x0[c // 2],
            "w_inproj": win_p,
            "w_outproj": arr("L0_attn_out_proj_w"),
            "w_ffn": arr("L0_ffn_weight"),
            "cos128": C["cos128"], "sin128": C["sin128"],
            "ropeB": C["ropeB"], "ident_b": C["ident_b"], "ident_f": C["ident_f"],
            "maskL0": C["maskL0"],
        })
    resA = run_bass_kernel_spmd(ncA, mapsA, core_ids=list(range(NCORE)))
    x2 = [resA.results[2 * bb]["x2"] for bb in range(4)]

    ncB = _get_built("B", build_l1a)
    mapsB = []
    for c in range(NCORE):
        bb_, hf = c // 2, c % 2
        Ch = Chalf[hf]
        mapsB.append({
            "x2r": np.ascontiguousarray(np.roll(x2[bb_], -2048 * hf, axis=0)),
            "Wq": arr("L1_attn_Wq"), "Wk": arr("L1_attn_Wk"),
            "Wv": arr("L1_attn_Wv"), "Wo": arr("L1_attn_Wo"),
            "brows": brows,
            "tkX": Ch["tkX"], "tvX": Ch["tvX"], "Mwrap": Ch["Mwrap"],
            "btm0": Ch["btm0"], "btm15": Ch["btm15"],
            "Mpast": C1["Mpast"], "Mfut": C1["Mfut"], "Shm": C1["Shm"],
            "ident_b": C["ident_b"], "ident_f": C["ident_f"],
        })
    resB = run_bass_kernel_spmd(ncB, mapsB, core_ids=list(range(NCORE)))
    x3 = [resB.results[c]["x3"] for c in range(NCORE)]

    ncC = _get_built("C", build_moe)
    W1f = np.ascontiguousarray(arr("L1_ffn_W1").reshape(8 * 1024, 4096))
    W2f = np.ascontiguousarray(arr("L1_ffn_W2").reshape(8 * 4096, 1024))
    mapsC = []
    for c in range(NCORE):
        mapsC.append({
            "x3": x3[c],
            "Wr": arr("L1_ffn_Wr"), "W1": W1f, "W2": W2f,
            "b1": arr("L1_ffn_b1"), "b2": arr("L1_ffn_b2"),
            "ident_b": C["ident_b"],
        })
    resC = run_bass_kernel_spmd(ncC, mapsC, core_ids=list(range(NCORE)))

    out = np.empty((4, 4096, 1024), np.float32)
    for c in range(NCORE):
        bb_, hf = c // 2, c % 2
        out[bb_, 2048 * hf:2048 * (hf + 1)] = resC.results[c]["x4"]
    return out
